# version 21
# baseline (speedup 1.0000x reference)
import numpy as np
import concourse.bass as bass
import concourse.mybir as mybir
from concourse.bass_utils import run_bass_kernel_spmd

F32 = mybir.dt.float32
BF16 = mybir.dt.bfloat16
I32 = mybir.dt.int32
AF = mybir.ActivationFunctionType
ALU = mybir.AluOpType
AX = mybir.AxisListType

NCORE = 8
BPC = 4
S = 2048
D = 1024
DC = D // 128
FF = 4096
FC = FF // 128
NG = S // 128
TCH = 256
NCH = S // TCH
EPS = 1e-5

ENGS = ("sync", "gpsimd", "tensor", "vector", "scalar")


def build_program():
    nc = bass.Bass("TRN2", target_bir_lowering=False, detect_race_conditions=False)

    x_tm = nc.dram_tensor("x_tm", [BPC, S, D], F32, kind="ExternalInput").ap()
    x_fm = nc.dram_tensor("x_fm", [BPC, D, S], F32, kind="ExternalInput").ap()
    WT = nc.dram_tensor("WT", [D, 3 * D], F32, kind="ExternalInput").ap()
    bqk = nc.dram_tensor("bqk", [128, 16], F32, kind="ExternalInput").ap()
    bv = nc.dram_tensor("bv", [1, D], F32, kind="ExternalInput").ap()
    gam = nc.dram_tensor("gam", [128, DC], F32, kind="ExternalInput").ap()
    bet = nc.dram_tensor("bet", [128, DC], F32, kind="ExternalInput").ap()
    q1 = nc.dram_tensor("q1", [D, FF], I32, kind="ExternalInput").ap()
    sc1 = nc.dram_tensor("sc1", [D, FF], F32, kind="ExternalInput").ap()
    b1 = nc.dram_tensor("b1", [128, FC], F32, kind="ExternalInput").ap()
    q2 = nc.dram_tensor("q2", [FF, D], I32, kind="ExternalInput").ap()
    sc2 = nc.dram_tensor("sc2", [FF, D], F32, kind="ExternalInput").ap()
    b2 = nc.dram_tensor("b2", [128, DC], F32, kind="ExternalInput").ap()
    ident_d = nc.dram_tensor("ident", [128, 128], F32, kind="ExternalInput").ap()
    out_fm = nc.dram_tensor("out_fm", [BPC, D, S], F32, kind="ExternalOutput").ap()

    wt_sp = nc.dram_tensor("wt_sp", [DC, 128, 3 * D], BF16, kind="Internal").ap()
    w1_sp = nc.dram_tensor("w1_sp", [DC, 128, FF], BF16, kind="Internal").ap()
    w2_sp = nc.dram_tensor("w2_sp", [DC, 128, FC * 128], BF16, kind="Internal").ap()
    at_sp = nc.dram_tensor("at_sp", [BPC, DC, 128, S], BF16, kind="Internal").ap()
    sr_sp = nc.dram_tensor("sr_sp", [BPC, 80, 128], F32, kind="Internal").ap()

    SB_END = nc.sbuf_top
    pos = [20480]

    def sb(name, shape, dtype):
        at = pos[0]
        per = int(np.prod(shape[1:])) * mybir.dt.size(dtype)
        pos[0] = (at + per + 31) & ~31
        assert pos[0] <= SB_END, f"sbuf overflow at {name}: {pos[0]}"
        return nc.alloc_sbuf_tensor_at(name, shape, dtype, offset=at)

    def sb_at(name, shape, dtype, at):
        per = int(np.prod(shape[1:])) * mybir.dt.size(dtype)
        assert at + per <= SB_END, f"sbuf overflow at {name}"
        return nc.alloc_sbuf_tensor_at(name, shape, dtype, offset=at)

    ident = sb("ident_sb", [128, 128], F32)
    bqk_c = sb("bqk_c", [128, 16], F32)
    gam_c = sb("gam_c", [128, DC], F32)
    bet_c = sb("bet_c", [128, DC], F32)
    b1_c = sb("b1_c", [128, FC], F32)
    bv_b = sb("bv_b", [128, D], BF16)
    halfb2 = sb("halfb2", [128, DC], F32)
    stat_c = sb("stat_c", [128, 80], F32)
    tmp96 = sb("tmp96", [96, 128], F32)
    scr = sb("scr", [128, 160], F32)
    W0 = (pos[0] + 31) & ~31

    RG = [sb_at(f"A_R{i}", [128, 32768], BF16, W0 + i * 65536) for i in range(2)]
    A_v = sb_at("A_v", [128, NG * D], BF16, W0 + 131072)
    A_xs0 = sb_at("A_xs0", [128, 1024], F32, W0 + 163840)
    A_xs1 = sb_at("A_xs1", [128, 1024], F32, W0 + 167936)
    A_rmb = sb_at("A_rmb", [128, 2048], F32, W0 + 172032)
    A_at0 = sb_at("A_at0", [128, S], BF16, W0 + 180224)
    A_at1 = sb_at("A_at1", [128, S], BF16, W0 + 184320)

    DQ_qa = sb_at("DQ_qa", [128, 512], I32, W0 + 188416)
    DQ_qb = sb_at("DQ_qb", [128, 512], I32, W0 + 190464)
    W_sa = sb_at("W_sa", [128, 512], F32, W0 + 188416)
    W_sb = sb_at("W_sb", [128, 512], F32, W0 + 190464)
    DQ_sa = sb_at("DQ_sa", [128, 512], F32, W0 + 192512)
    DQ_sb = sb_at("DQ_sb", [128, 512], F32, W0 + 194560)
    DQ_oa = sb_at("DQ_oa", [128, 512], BF16, W0 + 196608)
    DQ_ob = sb_at("DQ_ob", [128, 512], BF16, W0 + 197632)
    S_sq = sb_at("S_sq", [128, 1024], F32, W0 + 172032)
    assert W0 + 202752 <= SB_END

    F_w1 = sb_at("F_w1", [128, DC * FF], BF16, W0)
    F_w2 = sb_at("F_w2", [128, FC * D], BF16, W0 + 65536)
    F_yT = sb_at("F_yT", [128, FC * TCH], BF16, W0 + 131072)
    F_h0 = sb_at("F_h0", [128, DC * TCH], BF16, W0 + 198656)
    F_h1 = sb_at("F_h1", [128, DC * TCH], BF16, W0 + 151552)
    F_xh0 = sb_at("F_xh0", [128, DC * TCH], F32, W0 + 163840)
    F_xh1 = sb_at("F_xh1", [128, DC * TCH], F32, W0 + 155648)
    F_at0 = sb_at("F_at0", [128, DC * TCH], BF16, W0 + 172032)
    F_at1 = sb_at("F_at1", [128, DC * TCH], BF16, W0 + 176128)
    F_bc0 = sb_at("F_bc0", [128, 4 * TCH], F32, W0 + 180224)
    F_bc1 = sb_at("F_bc1", [128, 4 * TCH], F32, W0 + 184320)
    F_u0 = sb_at("F_u0", [128, TCH], F32, W0 + 188416)
    F_u1 = sb_at("F_u1", [128, TCH], F32, W0 + 189440)
    F_o = [sb_at(f"F_o{i}", [128, TCH], F32, W0 + 190464 + i * 1024) for i in range(4)]
    F_z0 = sb_at("F_z0", [128, TCH], F32, W0 + 194560)
    F_z1 = sb_at("F_z1", [128, TCH], F32, W0 + 195584)

    pb0 = nc.alloc_psum_tensor("pb0", [128, 2048], F32)
    pb1 = nc.alloc_psum_tensor("pb1", [128, 2048], F32)

    prog = {e: [] for e in ENGS}
    cnt = {"mm": 0, "vx": 0, "ac": 0, "gx": 0}
    dma_sems = set()

    def emit(eng, fn):
        prog[eng].append(fn)

    def wait(eng, *pairs):
        best = {}
        for sem, tok in pairs:
            if tok and tok > best.get(sem, 0):
                best[sem] = tok
        for sem, tok in best.items():
            emit(eng, lambda e, SS, s=sem, t=tok: e.wait_ge(SS[s], t))

    def dma(eng, sem, out_ap, in_ap, waits=()):
        wait(eng, *waits)
        dma_sems.add(sem)
        cnt[sem] = cnt.get(sem, 0) + 16
        tok = cnt[sem]
        emit(eng, lambda e, SS, o=out_ap, i=in_ap, s=sem: e.dma_start(out=o, in_=i).then_inc(SS[s], 16))
        return (sem, tok)

    def op(eng, fn, waits=(), sig=False):
        sem = {"tensor": "mm", "vector": "vx", "scalar": "ac", "gpsimd": "gx"}.get(eng)
        if eng in ("vector", "scalar", "gpsimd"):
            prev = cnt[sem]
            if prev:
                waits = tuple(waits) + ((sem, prev),)
            wait(eng, *waits)
            cnt[sem] += 1
            tok = cnt[sem]
            emit(eng, lambda e, SS, f=fn, s=sem: f(e).then_inc(SS[s], 1))
            return (sem, tok)
        wait(eng, *waits)
        if sig:
            cnt[sem] += 1
            tok = cnt[sem]
            emit(eng, lambda e, SS, f=fn, s=sem: f(e).then_inc(SS[s], 1))
            return (sem, tok)
        emit(eng, lambda e, SS, f=fn: f(e))
        return None

    psum_free = {}

    def psum_wait(*keys):
        return tuple(p for k in keys if (p := psum_free.get(k)) is not None)

    t_id = dma("sync", "s_init", ident[:], ident_d)
    t_bqk = dma("sync", "s_init", bqk_c[:], bqk)
    t_gam = dma("sync", "s_init", gam_c[:], gam)
    t_bet = dma("sync", "s_init", bet_c[:], bet)
    t_b1 = dma("sync", "s_init", b1_c[:], b1)
    t5 = dma("gpsimd", "g_init", bv_b[:], bass.AP(bv.tensor, 0, [[0, 128], [1, D]]))
    t6 = dma("sync", "s_init", halfb2[:], b2)
    INIT = ("s_init", cnt["s_init"])
    BVTOK = t5
    t_id = t_bqk = t_gam = t_bet = t_b1 = t6 = INIT
    cv1 = INIT
    cv2 = op("scalar", lambda e: e.mul(halfb2[:], halfb2[:], 0.5), waits=(t6,), sig=True)
    init_done = (cv2, t_id, t_bqk, t_gam, t_bet, t_b1)
    WT_r = WT.rearrange("(c p) f -> c p f", p=128)
    cast_v = dma("gpsimd", "g_wtv", wt_sp[:, :, 2 * D:3 * D], WT_r[:, :, 2 * D:3 * D])
    cast_q = dma("gpsimd", "g_wtq", wt_sp[:, :, 0:D], WT_r[:, :, 0:D])
    cast_k = dma("gpsimd", "g_wtk", wt_sp[:, :, D:2 * D], WT_r[:, :, D:2 * D])

    dq1_pieces = [("q1", dc, j) for dc in range(DC) for j in range(8)]
    dq2_pieces = [("q2", fc, j) for fc in range(FC) for j in range(2)]
    dq_all = dq1_pieces + dq2_pieces
    st_q = [DQ_qa, DQ_qb]
    st_w = [W_sa, W_sb]
    st_s = [DQ_sa, DQ_sb]
    st_o = [DQ_oa, DQ_ob]
    pc_state = {"idx": 0, "vec": [None, None], "spill": [None, None]}
    wt_spill_tok = {}
    w_spill_last = {"w1": [], "w2": []}

    def emit_piece(kind, a, j):
        i = pc_state["idx"]
        pi = i % 2
        war = pc_state["vec"][pi]
        war2 = pc_state["spill"][pi]
        if False:
            pass
        else:
            qsrc = q1 if kind == "q1" else q2
            ssrc = sc1 if kind == "q1" else sc2
            ldt = dma("sync", f"s_q{pi}", st_q[pi][:], qsrc[a * 128:(a + 1) * 128, j * 512:(j + 1) * 512],
                      waits=tuple(p for p in (war,) if p))
            ldt2 = dma("sync", f"s_sc{pi}", st_s[pi][:], ssrc[a * 128:(a + 1) * 128, j * 512:(j + 1) * 512],
                       waits=tuple(p for p in (war,) if p))
            vt = op("vector", lambda e, pi=pi: e.scalar_tensor_tensor(
                st_o[pi][:], st_q[pi][:], -128.0, st_s[pi][:], ALU.add, ALU.mult),
                waits=tuple(p for p in (ldt, ldt2, war2) if p), sig=True)
            if kind == "q1":
                wrt = dma("sync", f"g_sp{pi}", w1_sp[a, :, j * 512:(j + 1) * 512],
                          st_o[pi][:], waits=(vt,))
            else:
                wrt = dma("sync", f"g_sp{pi}",
                          w2_sp[j * 4:(j + 1) * 4, :, a * 128:(a + 1) * 128]
                          .rearrange("h p w -> p h w"),
                          st_o[pi][:].rearrange("p (h w) -> p h w", w=128), waits=(vt,))
            w_spill_last["w1" if kind == "q1" else "w2"].append(wrt)
        pc_state["vec"][pi] = vt
        pc_state["spill"][pi] = wrt
        pc_state["idx"] += 1

    xn_last = {}
    attnv_mm_last = {}
    attn_dma_last = {}
    attn_evac_prev = {}
    proj_mm_last = {}
    qk_evac_last = {}
    scores_mm_last = {}
    v_mm = {}
    st_done_t = {}
    trp_prev = {}
    tmp96_war = {}
    row_spill_tok = {}
    stats_last = {}
    stats_sc_last = {}
    xn_half = {}
    wtv_tok = {}
    wtq_tok = {}
    dq_cursor = [0]
    xs_war = [(), ()]
    rb_war_state = {}
    xs = [A_xs0, A_xs1]

    def reg(b):
        p = b % 2
        r = RG[p][:]
        q = RG[1 - p][:]
        return {
            "xn": r[:, 0:16384], "wtA": r[:, 16384:24576], "wtB": r[:, 24576:32768],
            "st": r, "qT": q[:, 0:16384], "kT": q[:, 16384:32768],
        }

    def emit_stats_group(b, g):
        gi = g % 2
        ldt = dma("sync", f"s_xs{gi}", xs[gi][:], x_tm[b, g * 128:(g + 1) * 128, :],
                  waits=xs_war[gi] + (init_done if (b == 0 and g < 2) else ()))
        ssc = op("scalar", lambda e, gi=gi, g=g: e.activation(
            S_sq[:], xs[gi][:], AF.Identity, accum_out=scr[:, g:g + 1]),
            waits=(ldt,), sig=True)
        last_sum = op("scalar", lambda e, gi=gi, g=g: e.activation(
            S_sq[:], xs[gi][:], AF.Square, accum_out=scr[:, 16 + g:17 + g]), sig=True)
        xs_war[gi] = (last_sum,)
        stats_last[b] = last_sum
        stats_sc_last[b] = last_sum
        return last_sum

    def emit_stats(b):
        sum_c = scr[:, 0:16]
        sq_c = scr[:, 16:32]
        m_c = scr[:, 32:48]
        e2_c = scr[:, 48:64]
        var_c = scr[:, 64:80]
        wA = scr[:, 128:144]
        wB = scr[:, 144:160]
        r_c = stat_c[:, 0:16]
        mr_c = stat_c[:, 16:32]
        rp1_c = stat_c[:, 32:48]
        rc_c = stat_c[:, 48:64]
        mrc_c = stat_c[:, 64:80]
        op("vector", lambda e: e.tensor_scalar_mul(m_c, sum_c, 1.0 / D),
           waits=tuple(p for p in (stats_sc_last.get(b),) if p))
        op("vector", lambda e: e.tensor_scalar_mul(e2_c, sq_c, 1.0 / D))
        op("vector", lambda e: e.tensor_mul(wA, m_c, m_c))
        op("vector", lambda e: e.tensor_sub(var_c, e2_c, wA))
        op("vector", lambda e: e.tensor_scalar_add(wA, var_c, EPS))
        v3 = op("vector", lambda e: e.tensor_copy(wB, wA), sig=True)

        def part_a3():
            st_state["a3"] = op("scalar", lambda e: e.activation(wB, wB, AF.Sqrt),
                                waits=(v3,), sig=True)

        def part_mid():
            v4w = (st_state["a3"],) + tuple(p for p in (trp_prev.get(b - 1),) if p)
            op("vector", lambda e: e.reciprocal(r_c, wB), waits=v4w)
            op("vector", lambda e: e.tensor_mul(mr_c, m_c, r_c))
            op("vector", lambda e: e.tensor_scalar_add(rp1_c, r_c, 1.0))
            op("vector", lambda e: e.tensor_mul(wA, var_c, r_c))
            op("vector", lambda e: e.tensor_mul(wA, wA, r_c))
            op("vector", lambda e: e.tensor_scalar_add(wA, wA, EPS))
            v5 = op("vector", lambda e: e.tensor_copy(wB, wA), sig=True)
            a4 = op("scalar", lambda e: e.activation(wB, wB, AF.Sqrt), waits=(v5,), sig=True)
            op("vector", lambda e: e.reciprocal(wA, wB), waits=(a4,))
            op("vector", lambda e: e.tensor_mul(rc_c, r_c, wA))
            st_done_t[b] = op("vector", lambda e: e.tensor_mul(mrc_c, mr_c, wA), sig=True)

        return part_a3, part_mid

    def emit_statchain(b):
        trp_w = (st_done_t[b],) + psum_wait(("pb1", 3)) + ((t_id,) if b == 0 else ())
        trp = op("tensor", lambda e: e.transpose(pb1[0:80, 1536:1664], stat_c[:, 0:80], ident[:]),
                 waits=trp_w, sig=True)
        trp_prev[b] = trp
        ev_w = (trp,) + tuple(p for p in (tmp96_war.get(b - 1),) if p)
        ev = op("scalar", lambda e: e.activation(tmp96[0:80, :], pb1[0:80, 1536:1664], AF.Identity),
                waits=ev_w, sig=True)
        psum_free[("pb1", 3)] = ev
        rA = dma("scalar", "s_sr", sr_sp[b, :, :], tmp96[0:80, :], waits=(ev,))
        tmp96_war[b] = rA
        row_spill_tok[b] = rA

    def emit_xn_wt(b):
        R = reg(b)
        reg_free = tuple(p for p in (scores_mm_last.get(b - 1),) if p)
        rbw = (row_spill_tok[b],) + tuple(
            p for p in (rb_war_state.get("w"), stats_last.get(b)) if p)
        rbd = dma("sync", "s_rb",
                  A_rmb[:].rearrange("p (q t) -> p q t", t=1024),
                  bass.AP(sr_sp.tensor, b * 10240, [[0, 128], [2048, 2], [1, 1024]]),
                  waits=rbw)
        xn_tok = None
        for h in range(2):
            for dc in range(DC):
                i = (h * DC + dc) % 2
                ldt = dma("sync", f"s_xs{i}", xs[i][:],
                          x_fm[b, dc * 128:(dc + 1) * 128, h * 1024:(h + 1) * 1024],
                          waits=xs_war[i])
                wts = (ldt, rbd) + (reg_free if (h == 0 and dc == 0) else ())
                op("vector", lambda e, i=i: e.tensor_mul(
                    xs[i][:], xs[i][:], A_rmb[:, 0:1024]), waits=wts)
                xn_tok = op("vector", lambda e, i=i, dc=dc, h=h, R=R: e.tensor_sub(
                    R["xn"][:, dc * 2048 + h * 1024: dc * 2048 + (h + 1) * 1024],
                    xs[i][:], A_rmb[:, 1024:2048]), sig=True)
                xs_war[i] = (xn_tok,)
            if h == 0:
                xn_half[b] = xn_tok
        rb_war_state["w"] = xn_tok
        xn_last[b] = xn_tok
        wtv_w = reg_free
        if b == 0:
            wtv_w = wtv_w + (cast_v,)
        wtv_tok[b] = dma("sync", "s_wtv", R["wtA"].rearrange("p (c f) -> p c f", f=1024),
                         wt_sp.rearrange("c p f -> p c f")[:, :, 2 * D:3 * D], waits=wtv_w)
        wtq_w = reg_free
        if b == 0:
            wtq_w = wtq_w + (cast_q,)
        wtq_tok[b] = dma("sync", "s_wtq", R["wtB"].rearrange("p (c f) -> p c f", f=1024),
                         wt_sp.rearrange("c p f -> p c f")[:, :, 0:D], waits=wtq_w)

    def emit_vproj(b):
        R = reg(b)
        ev_tok = last_v = None
        for tc in range(NG):
            for dvh in range(2):
                k = (tc * 2 + dvh) % 4
                pt = pb0[:, k * 512:(k + 1) * 512]
                mmw = psum_wait(("pb0", k))
                if tc == 0 and dvh == 0:
                    mmw = mmw + (xn_half[b], wtv_tok[b]) + ((INIT, BVTOK) if b == 0 else ())
                if tc == 8 and dvh == 0:
                    mmw = mmw + (xn_last[b],)
                for dc in range(DC):
                    last_v = op("tensor", lambda e, pt=pt, dc=dc, tc=tc, dvh=dvh, R=R: e.matmul(
                        pt, R["xn"][:, dc * 2048 + tc * 128: dc * 2048 + (tc + 1) * 128],
                        R["wtA"][:, dc * 1024 + dvh * 512: dc * 1024 + (dvh + 1) * 512],
                        start=(dc == 0), stop=(dc == DC - 1)),
                        waits=mmw if dc == 0 else (), sig=(dc == DC - 1))
                ev_tok = op("vector", lambda e, pt=pt, tc=tc, dvh=dvh: e.tensor_add(
                    A_v[:, tc * D + dvh * 512: tc * D + (dvh + 1) * 512], pt,
                    bv_b[:, dvh * 512:(dvh + 1) * 512]), waits=(last_v,), sig=True)
                psum_free[("pb0", k)] = ev_tok
        v_mm[b] = last_v

    def emit_qkproj(b):
        R = reg(b)
        wtk_w = (v_mm[b],)
        if b == 0:
            wtk_w = wtk_w + (cast_k,)
        wtk = dma("sync", "s_wtk", R["wtA"].rearrange("p (c f) -> p c f", f=1024),
                  wt_sp.rearrange("c p f -> p c f")[:, :, D:2 * D], waits=wtk_w)
        first_ev = True
        ev_tok = last = None
        for fc in range(16):
            wsl = R["wtB"] if fc < 8 else R["wtA"]
            dst = R["qT"] if fc < 8 else R["kT"]
            base = (fc % 8) * 2048
            for tt4 in range(4):
                k = (fc * 4 + tt4) % 4
                pt = pb0[:, k * 512:(k + 1) * 512]
                mmw = psum_wait(("pb0", k))
                if fc == 0 and tt4 == 0:
                    mmw = mmw + (wtq_tok[b],)
                if fc == 8 and tt4 == 0:
                    mmw = mmw + (wtk,)
                for dc in range(DC):
                    last = op("tensor", lambda e, pt=pt, dc=dc, fc=fc, tt4=tt4, wsl=wsl, R=R: e.matmul(
                        pt, wsl[:, dc * 1024 + (fc % 8) * 128: dc * 1024 + (fc % 8 + 1) * 128],
                        R["xn"][:, dc * 2048 + tt4 * 512: dc * 2048 + (tt4 + 1) * 512],
                        start=(dc == 0), stop=(dc == DC - 1)),
                        waits=mmw if dc == 0 else (), sig=(dc == DC - 1))
                ew = (last,)
                if first_ev:
                    ew = ew + tuple(p for p in (attnv_mm_last.get(b - 1),) if p)
                    if b == 0:
                        ew = ew + (t_bqk,)
                    first_ev = False
                ev_tok = op("scalar", lambda e, pt=pt, dst=dst, base=base, fc=fc, tt4=tt4: e.activation(
                    dst[:, base + tt4 * 512: base + (tt4 + 1) * 512], pt, AF.Identity,
                    bias=bqk_c[:, fc:fc + 1]), waits=ew, sig=True)
                psum_free[("pb0", k)] = ev_tok
        proj_mm_last[b] = last
        qk_evac_last[b] = ev_tok

    def emit_scores(b, stats_b=None):
        R = reg(b)
        exp_tok = vs_tok = last = None
        exp_l, vs_l = [], []
        for jc in range(NG):
            pt = pb1 if jc % 2 else pb0
            nm_ = "pb1" if jc % 2 else "pb0"
            mmw = psum_wait((nm_, 0), (nm_, 1), (nm_, 2), (nm_, 3))
            if jc == 0:
                mmw = mmw + (qk_evac_last[b],)
            for dc in range(DC):
                for ii in range(4):
                    last = op("tensor", lambda e, pt=pt, dc=dc, ii=ii, jc=jc, R=R: e.matmul(
                        pt[:, ii * 512:(ii + 1) * 512],
                        R["kT"][:, dc * 2048 + jc * 128: dc * 2048 + (jc + 1) * 128],
                        R["qT"][:, dc * 2048 + ii * 512: dc * 2048 + (ii + 1) * 512],
                        start=(dc == 0), stop=(dc == DC - 1)),
                        waits=mmw if (dc == 0 and ii == 0) else (), sig=(dc == DC - 1 and ii == 3))
            nm = scr[:, 80 + jc:81 + jc]
            sm = scr[:, 96 + jc:97 + jc]
            rq = scr[:, 112 + jc:113 + jc]
            op("vector", lambda e, pt=pt, nm=nm: e.reduce_max(nm, pt[:, 0:2048], axis=AX.X, negate=True),
               waits=(last,))
            mx2 = op("vector", lambda e, nm=nm: e.tensor_scalar_mul(nm, nm, 1.0 / D), sig=True)
            ew = (mx2,) + ((proj_mm_last[b],) if jc == 0 else ())
            exp_tok = op("scalar", lambda e, pt=pt, jc=jc, nm=nm, sm=sm, R=R: e.activation(
                R["st"][:, jc * 2048:(jc + 1) * 2048], pt[:, 0:2048], AF.Exp, bias=nm,
                scale=1.0 / D, accum_out=sm), waits=ew, sig=True)
            for k in range(4):
                psum_free[((nm_, k))] = exp_tok
            rc_ = op("vector", lambda e, sm=sm, rq=rq: e.reciprocal(rq, sm), waits=(exp_tok,), sig=True)
            vs_tok = op("scalar", lambda e, jc=jc, rq=rq: e.activation(
                A_v[:, jc * D:(jc + 1) * D], A_v[:, jc * D:(jc + 1) * D], AF.Identity, scale=rq),
                waits=(rc_,), sig=True)
            exp_l.append(exp_tok)
            vs_l.append(vs_tok)
            if stats_b is not None:
                emit_stats_group(stats_b, jc)
            for _ in range(2):
                if dq_cursor[0] < len(dq_all):
                    kind, a, j = dq_all[dq_cursor[0]]
                    emit_piece(kind, a, j)
                    dq_cursor[0] += 1
        scores_mm_last[b] = last
        return exp_l, vs_l

    def emit_attnv(b, dvcs, exp_l, vs_l, weave=None):
        R = reg(b)
        at_bufs = [A_at0, A_at1]
        last_mm = wrt = None
        for dvc in dvcs:
            if dvc % 2 == 0:
                sl = [pb0[:, 0:512], pb0[:, 512:1024], pb0[:, 1024:1536], pb0[:, 1536:2048]]
                keys = [("pb0", 0), ("pb0", 1), ("pb0", 2), ("pb0", 3)]
            else:
                sl = [pb1[:, 0:512], pb1[:, 512:1024], pb1[:, 1024:1536], pb0[:, 1536:2048]]
                keys = [("pb1", 0), ("pb1", 1), ("pb1", 2), ("pb0", 3)]
            mmw = psum_wait(*keys[:3])
            b3w = psum_wait(keys[3])
            for jc in range(NG):
                for ii in range(4):
                    w_ = mmw if (jc == 0 and ii == 0) else (b3w if (jc == 0 and ii == 3) else ())
                    if dvc == 0 and ii == 0:
                        w_ = w_ + (exp_l[jc], vs_l[jc])
                    last_mm = op("tensor", lambda e, sl=sl, dvc=dvc, jc=jc, ii=ii, R=R: e.matmul(
                        sl[ii], A_v[:, jc * D + dvc * 128: jc * D + (dvc + 1) * 128],
                        R["st"][:, jc * 2048 + ii * 512: jc * 2048 + (ii + 1) * 512],
                        start=(jc == 0), stop=(jc == NG - 1)),
                        waits=w_, sig=(jc == NG - 1 and ii == 3))
            abuf = at_bufs[dvc % 2]
            ew = (last_mm,) + tuple(p for p in (attn_evac_prev.get(dvc % 2),) if p)
            e1 = op("scalar", lambda e, sl=sl, abuf=abuf: e.activation(
                abuf[:, 1536:2048], sl[3], AF.Identity), waits=ew, sig=True)
            psum_free[keys[3]] = e1
            if dvc % 2 == 0:
                e2 = op("scalar", lambda e, abuf=abuf: e.activation(
                    abuf[:, 0:1536], pb0[:, 0:1536], AF.Identity), sig=True)
            else:
                e2 = op("scalar", lambda e, abuf=abuf: e.activation(
                    abuf[:, 0:1536], pb1[:, 0:1536], AF.Identity), sig=True)
            for kk in keys[:3]:
                psum_free[kk] = e2
            wrt = dma("scalar", f"g_at{dvc % 2}", at_sp[b, dvc, :, :], abuf[:], waits=(e2, e1))
            attn_evac_prev[dvc % 2] = wrt
            if weave and dvc in weave:
                for fn in weave[dvc]:
                    fn()
        return last_mm, wrt

    st_state = {}
    w1_state = {}

    def emit_w1_load():
        w1_state["tok"] = dma("scalar", "s_w1", F_w1[:].rearrange("p (c f) -> p c f", f=FF),
                              w1_sp.rearrange("c p f -> p c f"),
                              waits=(scores_mm_last[BPC - 1],) + tuple(w_spill_last["w1"]))

    for g in range(NG):
        emit_stats_group(0, g)
    pa, pm = emit_stats(0)
    pa(); pm()
    emit_statchain(0)
    emit_xn_wt(0)
    for b in range(BPC):
        emit_vproj(b)
        emit_qkproj(b)
        exp_l, vs_l = emit_scores(b, stats_b=(b + 1 if b + 1 < BPC else None))
        weave = {}
        if b + 1 < BPC:
            pa, pm = emit_stats(b + 1)
            weave = {0: [pa, pm], 1: [lambda bb=b + 1: emit_statchain(bb)]}
        if b == BPC - 1:
            emit_w1_load()
        emit_attnv(b, range(0, 4), exp_l, vs_l, weave=weave)
        mm2, wrt = emit_attnv(b, range(4, 8), exp_l, vs_l)
        attnv_mm_last[b] = mm2
        attn_dma_last[b] = wrt
        if b + 1 < BPC:
            emit_xn_wt(b + 1)

    b3 = BPC - 1
    w1_ld = w1_state["tok"]
    w2_ld = [dma("scalar", f"s_w2{hc}", F_w2[:, hc * FF:(hc + 1) * FF],
                 w2_sp[hc], waits=(attnv_mm_last[b3],) + tuple(w_spill_last["w2"]))
             for hc in range(DC)]
    f_at_war = (attn_evac_prev[0], attn_evac_prev[1])

    xh_war = [(xn_last[b3],), (attnv_mm_last[b3],)]
    h_war = [None, attnv_mm_last[b3]]
    at_war = [(xn_last[b3],)] * 2
    u_war = [None, None]
    o_war = [None, None, None, None]
    yT_war = attnv_mm_last[b3]
    silu_done = None
    h_done = None
    fc2_mm = {}

    z_war = [None, None]
    bc_war = [None, None]
    fstate = {}

    def f_front(ci):
        b, tc = ci // NCH, ci % NCH
        t0 = tc * TCH
        i2 = ci % 2
        xh = [F_xh0, F_xh1][i2]
        xld = dma("sync", f"s_xh{i2}", xh[:].rearrange("p (c j) -> p c j", j=TCH),
                  x_fm[b].rearrange("(c p) t -> p c t", p=128)[:, :, t0:t0 + TCH],
                  waits=xh_war[i2])
        at = [F_at0, F_at1][i2]
        ald = dma("sync", f"s_at{i2}", at[:].rearrange("p (c j) -> p c j", j=TCH),
                  at_sp.rearrange("b c p t -> b p c t")[b][:, :, t0:t0 + TCH],
                  waits=at_war[i2])
        bc = [F_bc0, F_bc1][i2]
        bcw = (row_spill_tok[b],) + tuple(p for p in (bc_war[i2],) if p)
        if ci < 2:
            bcw = bcw + (INIT, f_at_war[i2])
        bcd = dma("gpsimd", "g_rb", bc[:].rearrange("p (q t) -> p q t", t=TCH),
                  bass.AP(sr_sp.tensor, b * 10240 + 2048 + t0,
                          [[0, 128], [2048, 4], [1, TCH]]), waits=bcw)
        hbuf = [F_h0, F_h1][i2]
        h_done = None
        for dc in range(DC):
            ui = dc % 2
            ub = [F_u0, F_u1][ui]
            wts = ((xld, bcd) if dc == 0 else ())
            wts = wts + tuple(p for p in (u_war[ui],) if p)
            op("vector", lambda e, ub=ub, dc=dc, xh=xh, bc=bc: e.tensor_mul(
                ub[:], xh[:, dc * TCH:(dc + 1) * TCH], bc[:, 512:768]), waits=wts)
            vt = op("vector", lambda e, ub=ub, bc=bc: e.tensor_sub(ub[:], ub[:], bc[:, 768:1024]), sig=True)
            hw = (tuple(p for p in (h_war[i2],) if p) if dc == 0 else ())
            u_war[ui] = op("vector", lambda e, ub=ub, dc=dc, hbuf=hbuf: e.tensor_scalar(
                hbuf[:, dc * TCH:(dc + 1) * TCH], ub[:], gam_c[:, dc:dc + 1],
                bet_c[:, dc:dc + 1], ALU.mult, ALU.add), waits=hw, sig=True)
            h_done = u_war[ui]
        fstate[ci] = (xh, at, bc, hbuf, xld, ald, h_done)

    def f_back(ci):
        nonlocal yT_war
        b, tc = ci // NCH, ci % NCH
        t0 = tc * TCH
        i2 = ci % 2
        xh, at, bc, hbuf, xld, ald, h_done = fstate.pop(ci)
        silu_done = None
        for fc in range(FC):
            k = fc % 3
            pt = pb0[:, k * 512: k * 512 + TCH]
            mmw = psum_wait(("pb0", k)) + ((h_done, w1_ld) if fc == 0 else ())
            last = None
            for dc in range(DC):
                last = op("tensor", lambda e, pt=pt, fc=fc, dc=dc, hbuf=hbuf: e.matmul(
                    pt, F_w1[:, dc * FF + fc * 128: dc * FF + (fc + 1) * 128],
                    hbuf[:, dc * TCH:(dc + 1) * TCH],
                    start=(dc == 0), stop=(dc == DC - 1)),
                    waits=mmw if dc == 0 else (), sig=(dc == DC - 1))
            sw = (last,) + (tuple(p for p in (yT_war,) if p) if fc == 0 else ())
            if ci == 0 and fc == 0:
                sw = sw + (t_b1,)
            silu_done = op("scalar", lambda e, pt=pt, fc=fc: e.activation(
                F_yT[:, fc * TCH:(fc + 1) * TCH], pt, AF.Silu, bias=b1_c[:, fc:fc + 1]),
                waits=sw, sig=True)
            psum_free[("pb0", k)] = silu_done

        for hc in range(DC):
            k = hc % 3
            pt = pb1[:, k * 512: k * 512 + TCH]
            mmw = psum_wait(("pb1", k)) + ((silu_done,) if hc == 0 else ())
            if ci == 0:
                mmw = mmw + (w2_ld[hc],)
            last = None
            for fc in range(FC):
                last = op("tensor", lambda e, pt=pt, hc=hc, fc=fc: e.matmul(
                    pt, F_w2[:, hc * FF + fc * 128: hc * FF + (fc + 1) * 128],
                    F_yT[:, fc * TCH:(fc + 1) * TCH],
                    start=(fc == 0), stop=(fc == FC - 1)), waits=mmw if fc == 0 else (),
                    sig=(fc == FC - 1))
            fc2_mm[hc] = last
            zi = hc % 2
            zb = [F_z0, F_z1][zi]
            zw = (last,) + tuple(p for p in (z_war[zi],) if p)
            z_tok = op("scalar", lambda e, zb=zb, pt=pt, hc=hc: e.activation(
                zb[:], pt, AF.Identity, scale=0.5, bias=halfb2[:, hc:hc + 1]),
                waits=zw, sig=True)
            psum_free[("pb1", k)] = z_tok
            oi = hc % 4
            ob = F_o[oi]
            ow = tuple(p for p in (o_war[oi],) if p)
            op("vector", lambda e, ob=ob, hc=hc, xh=xh, bc=bc: e.tensor_mul(
                ob[:], xh[:, hc * TCH:(hc + 1) * TCH], bc[:, 256:512]), waits=ow)
            op("vector", lambda e, ob=ob, bc=bc: e.tensor_sub(ob[:], ob[:], bc[:, 0:256]))
            aw = (ald,) if hc == 0 else ()
            op("vector", lambda e, ob=ob, hc=hc, at=at: e.tensor_add(
                ob[:], ob[:], at[:, hc * TCH:(hc + 1) * TCH]), waits=aw)
            o4 = op("vector", lambda e, ob=ob, zb=zb: e.tensor_add(ob[:], ob[:], zb[:]),
                    waits=(z_tok,), sig=True)
            z_war[zi] = o4
            o_war[oi] = dma("gpsimd", f"g_o{oi}", out_fm[b, hc * 128:(hc + 1) * 128, t0:t0 + TCH],
                            ob[:], waits=(o4,))
            if hc == DC - 1:
                xh_war[i2] = (o4,)
                at_war[i2] = (o4,)
                bc_war[i2] = o4
        yT_war = fc2_mm[DC - 1]
        h_war[i2] = fc2_mm[DC - 1]

    f_front(0)
    for ci in range(BPC * NCH):
        if ci + 1 < BPC * NCH:
            f_front(ci + 1)
        f_back(ci)

    for snm in sorted(dma_sems):
        if snm.startswith("g_"):
            emit("gpsimd", lambda e, SS, s=snm, t=cnt[snm]: e.wait_ge(SS[s], t))

    sems = {}
    for snm in sorted(dma_sems) + ["mm", "vx", "ac", "gx"]:
        sems[snm] = nc.alloc_semaphore(name="sem_" + snm)
    with nc.Block() as blk:

        @blk.sync
        def _(e):
            for f in prog["sync"]:
                f(e, sems)

        @blk.gpsimd
        def _(e):
            for f in prog["gpsimd"]:
                f(e, sems)

        @blk.tensor
        def _(e):
            for f in prog["tensor"]:
                f(e, sems)

        @blk.vector
        def _(e):
            for f in prog["vector"]:
                f(e, sems)

        @blk.scalar
        def _(e):
            for f in prog["scalar"]:
                f(e, sems)

    return nc


def kernel(L0_attn_x, L0_attn_W, L0_attn_b, L0_ffn_ln_gamma, L0_ffn_ln_beta,
           L0_ffn_fc1_q, L0_ffn_fc1_scales, L0_ffn_fc1_bias, L0_ffn_fc2_q,
           L0_ffn_fc2_scales, L0_ffn_fc2_bias):
    x = np.asarray(L0_attn_x)
    W = np.asarray(L0_attn_W)
    bqk_full = np.asarray(L0_attn_b)
    WT = np.ascontiguousarray(W.T)
    bqk = np.ascontiguousarray(bqk_full[0:2048].reshape(16, 128).T)
    bv = np.ascontiguousarray(bqk_full[2048:3072].reshape(1, 1024))
    gam = np.ascontiguousarray(np.asarray(L0_ffn_ln_gamma).reshape(DC, 128).T)
    bet = np.ascontiguousarray(np.asarray(L0_ffn_ln_beta).reshape(DC, 128).T)
    q1 = np.ascontiguousarray(np.asarray(L0_ffn_fc1_q))
    sc1 = np.ascontiguousarray(np.repeat(np.asarray(L0_ffn_fc1_scales), 128, axis=0))
    b1 = np.ascontiguousarray(np.asarray(L0_ffn_fc1_bias).reshape(FC, 128).T)
    q2 = np.ascontiguousarray(np.asarray(L0_ffn_fc2_q))
    sc2 = np.ascontiguousarray(np.repeat(np.asarray(L0_ffn_fc2_scales), 128, axis=0))
    b2 = np.ascontiguousarray(np.asarray(L0_ffn_fc2_bias).reshape(DC, 128).T)
    ident = np.eye(128, dtype=np.float32)

    nc = build_program()
    in_maps = []
    for c in range(NCORE):
        xs = x[c * BPC:(c + 1) * BPC]
        in_maps.append({
            "x_tm": np.ascontiguousarray(xs),
            "x_fm": np.ascontiguousarray(xs.transpose(0, 2, 1)),
            "WT": WT, "bqk": bqk, "bv": bv, "gam": gam, "bet": bet,
            "q1": q1, "sc1": sc1, "b1": b1, "q2": q2, "sc2": sc2, "b2": b2,
            "ident": ident,
        })
    res = run_bass_kernel_spmd(nc, in_maps, core_ids=list(range(NCORE)))
    outs = [r["out_fm"].transpose(0, 2, 1) for r in res.results]
    return np.ascontiguousarray(np.concatenate(outs, axis=0))


# version 22
# speedup vs baseline: 1.1078x; 1.1078x over previous
import numpy as np
import concourse.bass as bass
import concourse.mybir as mybir
from concourse.bass_utils import run_bass_kernel_spmd

F32 = mybir.dt.float32
BF16 = mybir.dt.bfloat16
I32 = mybir.dt.int32
AF = mybir.ActivationFunctionType
ALU = mybir.AluOpType
AX = mybir.AxisListType

NCORE = 8
BPC = 4
S = 2048
D = 1024
DC = D // 128
FF = 4096
FC = FF // 128
NG = S // 128
TCH = 256
NCH = S // TCH
EPS = 1e-5

ENGS = ("sync", "gpsimd", "tensor", "vector", "scalar")


def build_program():
    nc = bass.Bass("TRN2", target_bir_lowering=False, detect_race_conditions=False)

    x_tm = nc.dram_tensor("x_tm", [BPC, S, D], F32, kind="ExternalInput").ap()
    x_fm = nc.dram_tensor("x_fm", [BPC, D, S], F32, kind="ExternalInput").ap()
    WT = nc.dram_tensor("WT", [D, 3 * D], F32, kind="ExternalInput").ap()
    bqk = nc.dram_tensor("bqk", [128, 16], F32, kind="ExternalInput").ap()
    bv = nc.dram_tensor("bv", [1, D], F32, kind="ExternalInput").ap()
    gam = nc.dram_tensor("gam", [128, DC], F32, kind="ExternalInput").ap()
    bet = nc.dram_tensor("bet", [128, DC], F32, kind="ExternalInput").ap()
    q1 = nc.dram_tensor("q1", [D, FF], I32, kind="ExternalInput").ap()
    sc1 = nc.dram_tensor("sc1", [D, FF], F32, kind="ExternalInput").ap()
    b1 = nc.dram_tensor("b1", [128, FC], F32, kind="ExternalInput").ap()
    q2 = nc.dram_tensor("q2", [FF, D], I32, kind="ExternalInput").ap()
    sc2 = nc.dram_tensor("sc2", [FF, D], F32, kind="ExternalInput").ap()
    b2 = nc.dram_tensor("b2", [128, DC], F32, kind="ExternalInput").ap()
    ident_d = nc.dram_tensor("ident", [128, 128], F32, kind="ExternalInput").ap()
    out_fm = nc.dram_tensor("out_fm", [BPC, D, S], F32, kind="ExternalOutput").ap()

    wt_sp = nc.dram_tensor("wt_sp", [DC, 128, 3 * D], BF16, kind="Internal").ap()
    w1_sp = nc.dram_tensor("w1_sp", [DC, 128, FF], BF16, kind="Internal").ap()
    w2_sp = nc.dram_tensor("w2_sp", [DC, 128, FC * 128], BF16, kind="Internal").ap()
    at_sp = nc.dram_tensor("at_sp", [BPC, DC, 128, S], BF16, kind="Internal").ap()
    sr_sp = nc.dram_tensor("sr_sp", [BPC, 80, 128], F32, kind="Internal").ap()

    SB_END = nc.sbuf_top
    pos = [20480]

    def sb(name, shape, dtype):
        at = pos[0]
        per = int(np.prod(shape[1:])) * mybir.dt.size(dtype)
        pos[0] = (at + per + 31) & ~31
        assert pos[0] <= SB_END, f"sbuf overflow at {name}: {pos[0]}"
        return nc.alloc_sbuf_tensor_at(name, shape, dtype, offset=at)

    def sb_at(name, shape, dtype, at):
        per = int(np.prod(shape[1:])) * mybir.dt.size(dtype)
        assert at + per <= SB_END, f"sbuf overflow at {name}"
        return nc.alloc_sbuf_tensor_at(name, shape, dtype, offset=at)

    ident = sb("ident_sb", [128, 128], F32)
    bqk_c = sb("bqk_c", [128, 16], F32)
    gam_c = sb("gam_c", [128, DC], F32)
    bet_c = sb("bet_c", [128, DC], F32)
    b1_c = sb("b1_c", [128, FC], F32)
    bv_b = sb("bv_b", [128, D], BF16)
    halfb2 = sb("halfb2", [128, DC], F32)
    stat_c = sb("stat_c", [128, 80], F32)
    tmp96 = sb("tmp96", [96, 128], F32)
    scr = sb("scr", [128, 160], F32)
    W0 = (pos[0] + 31) & ~31

    RG = [sb_at(f"A_R{i}", [128, 32768], BF16, W0 + i * 65536) for i in range(2)]
    A_v = sb_at("A_v", [128, NG * D], BF16, W0 + 131072)
    A_xs0 = sb_at("A_xs0", [128, 1024], F32, W0 + 163840)
    A_xs1 = sb_at("A_xs1", [128, 1024], F32, W0 + 167936)
    A_rmb = sb_at("A_rmb", [128, 2048], F32, W0 + 172032)
    A_at0 = sb_at("A_at0", [128, S], BF16, W0 + 180224)
    A_at1 = sb_at("A_at1", [128, S], BF16, W0 + 184320)

    DQ_qa = sb_at("DQ_qa", [128, 512], I32, W0 + 188416)
    DQ_qb = sb_at("DQ_qb", [128, 512], I32, W0 + 190464)
    W_sa = sb_at("W_sa", [128, 512], F32, W0 + 188416)
    W_sb = sb_at("W_sb", [128, 512], F32, W0 + 190464)
    DQ_sa = sb_at("DQ_sa", [128, 512], F32, W0 + 192512)
    DQ_sb = sb_at("DQ_sb", [128, 512], F32, W0 + 194560)
    DQ_oa = sb_at("DQ_oa", [128, 512], BF16, W0 + 196608)
    DQ_ob = sb_at("DQ_ob", [128, 512], BF16, W0 + 197632)
    S_sq = sb_at("S_sq", [128, 1024], F32, W0 + 172032)
    assert W0 + 202752 <= SB_END

    F_w1 = sb_at("F_w1", [128, DC * FF], BF16, W0)
    F_w2 = sb_at("F_w2", [128, FC * D], BF16, W0 + 65536)
    F_yT = sb_at("F_yT", [128, FC * TCH], BF16, W0 + 131072)
    F_h0 = sb_at("F_h0", [128, DC * TCH], BF16, W0 + 198656)
    F_h1 = sb_at("F_h1", [128, DC * TCH], BF16, W0 + 151552)
    F_xh0 = sb_at("F_xh0", [128, DC * TCH], F32, W0 + 163840)
    F_xh1 = sb_at("F_xh1", [128, DC * TCH], F32, W0 + 155648)
    F_at0 = sb_at("F_at0", [128, DC * TCH], BF16, W0 + 172032)
    F_at1 = sb_at("F_at1", [128, DC * TCH], BF16, W0 + 176128)
    F_bc0 = sb_at("F_bc0", [128, 4 * TCH], F32, W0 + 180224)
    F_bc1 = sb_at("F_bc1", [128, 4 * TCH], F32, W0 + 184320)
    F_u0 = sb_at("F_u0", [128, TCH], F32, W0 + 188416)
    F_u1 = sb_at("F_u1", [128, TCH], F32, W0 + 189440)
    F_o = [sb_at(f"F_o{i}", [128, TCH], F32, W0 + 190464 + i * 1024) for i in range(4)]
    F_z0 = sb_at("F_z0", [128, TCH], F32, W0 + 194560)
    F_z1 = sb_at("F_z1", [128, TCH], F32, W0 + 195584)

    pb0 = nc.alloc_psum_tensor("pb0", [128, 2048], F32)
    pb1 = nc.alloc_psum_tensor("pb1", [128, 2048], F32)

    prog = {e: [] for e in ENGS}
    cnt = {"mm": 0, "vx": 0, "ac": 0}
    dma_sems = set()

    def emit(eng, fn):
        prog[eng].append(fn)

    def _dedupe(pairs):
        best = {}
        for sem, tok in pairs:
            if tok and tok > best.get(sem, 0):
                best[sem] = tok
        return list(best.items())

    def wait(eng, *pairs):
        for sem, tok in _dedupe(pairs):
            emit(eng, lambda e, SS, s=sem, t=tok: e.wait_ge(SS[s], t))

    def _prewaits(eng, pairs):
        items = _dedupe(pairs)
        for sem, tok in items[:-1]:
            emit(eng, lambda e, SS, s=sem, t=tok: e.wait_ge(SS[s], t))
        return items[-1] if items else None

    def dma(eng, sem, out_ap, in_ap, waits=()):
        att = _prewaits(eng, waits)
        dma_sems.add(sem)
        cnt[sem] = cnt.get(sem, 0) + 16
        tok = cnt[sem]

        def go(e, SS, o=out_ap, i=in_ap, s=sem, att=att):
            inst = e.dma_start(out=o, in_=i)
            if att is not None:
                inst._wait_ge(SS[att[0]], att[1])
            inst.then_inc(SS[s], 16)
        emit(eng, go)
        return (sem, tok)

    def op(eng, fn, waits=(), sig=False):
        sem = {"tensor": "mm", "vector": "vx", "scalar": "ac"}.get(eng)
        if eng in ("vector", "scalar"):
            prev = cnt[sem]
            if prev:
                waits = tuple(waits) + ((sem, prev),)
            att = _prewaits(eng, waits)
            cnt[sem] += 1
            tok = cnt[sem]

            def go(e, SS, f=fn, s=sem, att=att):
                inst = f(e)
                if att is not None:
                    inst._wait_ge(SS[att[0]], att[1])
                inst.then_inc(SS[s], 1)
            emit(eng, go)
            return (sem, tok)
        att = _prewaits(eng, waits)
        if sig:
            cnt[sem] += 1
            tok = cnt[sem]

            def go(e, SS, f=fn, s=sem, att=att):
                inst = f(e)
                if att is not None:
                    inst._wait_ge(SS[att[0]], att[1])
                inst.then_inc(SS[s], 1)
            emit(eng, go)
            return (sem, tok)

        def go2(e, SS, f=fn, att=att):
            inst = f(e)
            if att is not None:
                inst._wait_ge(SS[att[0]], att[1])
        emit(eng, go2)
        return None

    psum_free = {}

    def psum_wait(*keys):
        return tuple(p for k in keys if (p := psum_free.get(k)) is not None)

    t_id = dma("sync", "s_init", ident[:], ident_d)
    t_bqk = dma("sync", "s_init", bqk_c[:], bqk)
    t_gam = dma("sync", "s_init", gam_c[:], gam)
    t_bet = dma("sync", "s_init", bet_c[:], bet)
    t_b1 = dma("sync", "s_init", b1_c[:], b1)
    t5 = dma("gpsimd", "g_init", bv_b[:], bass.AP(bv.tensor, 0, [[0, 128], [1, D]]))
    t6 = dma("sync", "s_init", halfb2[:], b2)
    INIT = ("s_init", cnt["s_init"])
    BVTOK = t5
    t_id = t_bqk = t_gam = t_bet = t_b1 = t6 = INIT
    cv1 = INIT
    cv2 = op("scalar", lambda e: e.mul(halfb2[:], halfb2[:], 0.5), waits=(t6,), sig=True)
    init_done = (cv2, t_id, t_bqk, t_gam, t_bet, t_b1)
    WT_r = WT.rearrange("(c p) f -> c p f", p=128)
    cast_v = dma("gpsimd", "g_wtv", wt_sp[:, :, 2 * D:3 * D], WT_r[:, :, 2 * D:3 * D])
    cast_q = dma("gpsimd", "g_wtq", wt_sp[:, :, 0:D], WT_r[:, :, 0:D])
    cast_k = dma("gpsimd", "g_wtk", wt_sp[:, :, D:2 * D], WT_r[:, :, D:2 * D])

    dq1_pieces = [("q1", dc, j) for dc in range(DC) for j in range(8)]
    dq2_pieces = [("q2", fc, j) for fc in range(FC) for j in range(2)]
    dq_all = dq1_pieces + dq2_pieces
    st_q = [DQ_qa, DQ_qb]
    st_w = [W_sa, W_sb]
    st_s = [DQ_sa, DQ_sb]
    st_o = [DQ_oa, DQ_ob]
    pc_state = {"idx": 0, "vec": [None, None], "spill": [None, None]}
    wt_spill_tok = {}
    w_spill_last = {"w1": [], "w2": []}

    def emit_piece(kind, a, j):
        i = pc_state["idx"]
        pi = i % 2
        war = pc_state["vec"][pi]
        war2 = pc_state["spill"][pi]
        if False:
            pass
        else:
            qsrc = q1 if kind == "q1" else q2
            ssrc = sc1 if kind == "q1" else sc2
            ldt = dma("sync", f"s_q{pi}", st_q[pi][:], qsrc[a * 128:(a + 1) * 128, j * 512:(j + 1) * 512],
                      waits=tuple(p for p in (war,) if p))
            ldt2 = dma("sync", f"s_sc{pi}", st_s[pi][:], ssrc[a * 128:(a + 1) * 128, j * 512:(j + 1) * 512],
                       waits=tuple(p for p in (war,) if p))
            vt = op("vector", lambda e, pi=pi: e.scalar_tensor_tensor(
                st_o[pi][:], st_q[pi][:], -128.0, st_s[pi][:], ALU.add, ALU.mult),
                waits=tuple(p for p in (ldt, ldt2, war2) if p), sig=True)
            if kind == "q1":
                wrt = dma("gpsimd", f"g_sp{pi}", w1_sp[a, :, j * 512:(j + 1) * 512],
                          st_o[pi][:], waits=(vt,))
            else:
                wrt = dma("gpsimd", f"g_sp{pi}",
                          w2_sp[j * 4:(j + 1) * 4, :, a * 128:(a + 1) * 128]
                          .rearrange("h p w -> p h w"),
                          st_o[pi][:].rearrange("p (h w) -> p h w", w=128), waits=(vt,))
            w_spill_last["w1" if kind == "q1" else "w2"].append(wrt)
        pc_state["vec"][pi] = vt
        pc_state["spill"][pi] = wrt
        pc_state["idx"] += 1

    xn_last = {}
    attnv_mm_last = {}
    attn_dma_last = {}
    attn_evac_prev = {}
    proj_mm_last = {}
    qk_evac_last = {}
    scores_mm_last = {}
    v_mm = {}
    st_done_t = {}
    trp_prev = {}
    tmp96_war = {}
    row_spill_tok = {}
    stats_last = {}
    stats_sc_last = {}
    xn_half = {}
    wtv_tok = {}
    wtq_tok = {}
    dq_cursor = [0]
    xs_war = [(), ()]
    rb_war_state = {}
    xs = [A_xs0, A_xs1]

    def reg(b):
        p = b % 2
        r = RG[p][:]
        q = RG[1 - p][:]
        return {
            "xn": r[:, 0:16384], "wtA": r[:, 16384:24576], "wtB": r[:, 24576:32768],
            "st": r, "qT": q[:, 0:16384], "kT": q[:, 16384:32768],
        }

    def emit_stats_group(b, g):
        gi = g % 2
        ldt = dma("sync", f"s_xs{gi}", xs[gi][:], x_tm[b, g * 128:(g + 1) * 128, :],
                  waits=xs_war[gi] + (init_done if (b == 0 and g < 2) else ()))
        ssc = op("scalar", lambda e, gi=gi, g=g: e.activation(
            S_sq[:], xs[gi][:], AF.Identity, accum_out=scr[:, g:g + 1]),
            waits=(ldt,), sig=True)
        last_sum = op("vector", lambda e, gi=gi, g=g: e.scalar_tensor_tensor(
            S_sq[:], xs[gi][:], 1.0, xs[gi][:], ALU.mult, ALU.mult,
            accum_out=scr[:, 16 + g:17 + g]), waits=(ldt,), sig=True)
        xs_war[gi] = (last_sum, ssc)
        stats_last[b] = last_sum
        stats_sc_last[b] = ssc
        return last_sum

    def emit_stats(b):
        sum_c = scr[:, 0:16]
        sq_c = scr[:, 16:32]
        m_c = scr[:, 32:48]
        e2_c = scr[:, 48:64]
        var_c = scr[:, 64:80]
        wA = scr[:, 128:144]
        wB = scr[:, 144:160]
        r_c = stat_c[:, 0:16]
        mr_c = stat_c[:, 16:32]
        rp1_c = stat_c[:, 32:48]
        rc_c = stat_c[:, 48:64]
        mrc_c = stat_c[:, 64:80]
        op("vector", lambda e: e.tensor_scalar_mul(m_c, sum_c, 1.0 / D),
           waits=tuple(p for p in (stats_sc_last.get(b),) if p))
        op("vector", lambda e: e.tensor_scalar_mul(e2_c, sq_c, 1.0 / D))
        op("vector", lambda e: e.tensor_mul(wA, m_c, m_c))
        op("vector", lambda e: e.tensor_sub(var_c, e2_c, wA))
        op("vector", lambda e: e.tensor_scalar_add(wA, var_c, EPS))
        v3 = op("vector", lambda e: e.tensor_copy(wB, wA), sig=True)

        def part_a3():
            st_state["a3"] = op("scalar", lambda e: e.activation(wB, wB, AF.Sqrt),
                                waits=(v3,), sig=True)

        def part_mid():
            v4w = (st_state["a3"],) + tuple(p for p in (trp_prev.get(b - 1),) if p)
            op("vector", lambda e: e.reciprocal(r_c, wB), waits=v4w)
            op("vector", lambda e: e.tensor_mul(mr_c, m_c, r_c))
            op("vector", lambda e: e.tensor_scalar_add(rp1_c, r_c, 1.0))
            op("vector", lambda e: e.tensor_mul(wA, var_c, r_c))
            op("vector", lambda e: e.tensor_mul(wA, wA, r_c))
            op("vector", lambda e: e.tensor_scalar_add(wA, wA, EPS))
            v5 = op("vector", lambda e: e.tensor_copy(wB, wA), sig=True)
            a4 = op("scalar", lambda e: e.activation(wB, wB, AF.Sqrt), waits=(v5,), sig=True)
            op("vector", lambda e: e.reciprocal(wA, wB), waits=(a4,))
            op("vector", lambda e: e.tensor_mul(rc_c, r_c, wA))
            st_done_t[b] = op("vector", lambda e: e.tensor_mul(mrc_c, mr_c, wA), sig=True)

        return part_a3, part_mid

    def emit_statchain(b):
        trp_w = (st_done_t[b],) + psum_wait(("pb1", 3)) + ((t_id,) if b == 0 else ())
        trp = op("tensor", lambda e: e.transpose(pb1[0:80, 1536:1664], stat_c[:, 0:80], ident[:]),
                 waits=trp_w, sig=True)
        trp_prev[b] = trp
        ev_w = (trp,) + tuple(p for p in (tmp96_war.get(b - 1),) if p)
        ev = op("scalar", lambda e: e.activation(tmp96[0:80, :], pb1[0:80, 1536:1664], AF.Identity),
                waits=ev_w, sig=True)
        psum_free[("pb1", 3)] = ev
        rA = dma("scalar", "s_sr", sr_sp[b, :, :], tmp96[0:80, :], waits=(ev,))
        tmp96_war[b] = rA
        row_spill_tok[b] = rA

    def emit_xn_wt(b):
        R = reg(b)
        reg_free = tuple(p for p in (scores_mm_last.get(b - 1),) if p)
        rbw = (row_spill_tok[b],) + tuple(
            p for p in (rb_war_state.get("w"), stats_last.get(b)) if p)
        rbd = dma("sync", "s_rb",
                  A_rmb[:].rearrange("p (q t) -> p q t", t=1024),
                  bass.AP(sr_sp.tensor, b * 10240, [[0, 128], [2048, 2], [1, 1024]]),
                  waits=rbw)
        xn_tok = None
        for h in range(2):
            for dc in range(DC):
                i = (h * DC + dc) % 2
                ldt = dma("sync", f"s_xs{i}", xs[i][:],
                          x_fm[b, dc * 128:(dc + 1) * 128, h * 1024:(h + 1) * 1024],
                          waits=xs_war[i])
                wts = (ldt, rbd) + (reg_free if (h == 0 and dc == 0) else ())
                op("vector", lambda e, i=i: e.tensor_mul(
                    xs[i][:], xs[i][:], A_rmb[:, 0:1024]), waits=wts)
                xn_tok = op("vector", lambda e, i=i, dc=dc, h=h, R=R: e.tensor_sub(
                    R["xn"][:, dc * 2048 + h * 1024: dc * 2048 + (h + 1) * 1024],
                    xs[i][:], A_rmb[:, 1024:2048]), sig=True)
                xs_war[i] = (xn_tok,)
            if h == 0:
                xn_half[b] = xn_tok
        rb_war_state["w"] = xn_tok
        xn_last[b] = xn_tok
        wtv_w = reg_free
        if b == 0:
            wtv_w = wtv_w + (cast_v,)
        wtv_tok[b] = dma("sync", "s_wtv", R["wtA"].rearrange("p (c f) -> p c f", f=1024),
                         wt_sp.rearrange("c p f -> p c f")[:, :, 2 * D:3 * D], waits=wtv_w)
        wtq_w = reg_free
        if b == 0:
            wtq_w = wtq_w + (cast_q,)
        wtq_tok[b] = dma("sync", "s_wtq", R["wtB"].rearrange("p (c f) -> p c f", f=1024),
                         wt_sp.rearrange("c p f -> p c f")[:, :, 0:D], waits=wtq_w)

    def emit_vproj(b):
        R = reg(b)
        ev_tok = last_v = None
        for tc in range(NG):
            for dvh in range(2):
                k = (tc * 2 + dvh) % 4
                pt = pb0[:, k * 512:(k + 1) * 512]
                mmw = psum_wait(("pb0", k))
                if tc == 0 and dvh == 0:
                    mmw = mmw + (xn_half[b], wtv_tok[b]) + ((INIT, BVTOK) if b == 0 else ())
                if tc == 8 and dvh == 0:
                    mmw = mmw + (xn_last[b],)
                for dc in range(DC):
                    last_v = op("tensor", lambda e, pt=pt, dc=dc, tc=tc, dvh=dvh, R=R: e.matmul(
                        pt, R["xn"][:, dc * 2048 + tc * 128: dc * 2048 + (tc + 1) * 128],
                        R["wtA"][:, dc * 1024 + dvh * 512: dc * 1024 + (dvh + 1) * 512],
                        start=(dc == 0), stop=(dc == DC - 1)),
                        waits=mmw if dc == 0 else (), sig=(dc == DC - 1))
                ev_tok = op("vector", lambda e, pt=pt, tc=tc, dvh=dvh: e.tensor_add(
                    A_v[:, tc * D + dvh * 512: tc * D + (dvh + 1) * 512], pt,
                    bv_b[:, dvh * 512:(dvh + 1) * 512]), waits=(last_v,), sig=True)
                psum_free[("pb0", k)] = ev_tok
        v_mm[b] = last_v

    def emit_qkproj(b):
        R = reg(b)
        wtk_w = (v_mm[b],)
        if b == 0:
            wtk_w = wtk_w + (cast_k,)
        wtk = dma("sync", "s_wtk", R["wtA"].rearrange("p (c f) -> p c f", f=1024),
                  wt_sp.rearrange("c p f -> p c f")[:, :, D:2 * D], waits=wtk_w)
        first_ev = True
        ev_tok = last = None
        for fc in range(16):
            wsl = R["wtB"] if fc < 8 else R["wtA"]
            dst = R["qT"] if fc < 8 else R["kT"]
            base = (fc % 8) * 2048
            for tt4 in range(4):
                k = (fc * 4 + tt4) % 4
                pt = pb0[:, k * 512:(k + 1) * 512]
                mmw = psum_wait(("pb0", k))
                if fc == 0 and tt4 == 0:
                    mmw = mmw + (wtq_tok[b],)
                if fc == 8 and tt4 == 0:
                    mmw = mmw + (wtk,)
                for dc in range(DC):
                    last = op("tensor", lambda e, pt=pt, dc=dc, fc=fc, tt4=tt4, wsl=wsl, R=R: e.matmul(
                        pt, wsl[:, dc * 1024 + (fc % 8) * 128: dc * 1024 + (fc % 8 + 1) * 128],
                        R["xn"][:, dc * 2048 + tt4 * 512: dc * 2048 + (tt4 + 1) * 512],
                        start=(dc == 0), stop=(dc == DC - 1)),
                        waits=mmw if dc == 0 else (), sig=(dc == DC - 1))
                ew = (last,)
                if first_ev:
                    ew = ew + tuple(p for p in (attnv_mm_last.get(b - 1),) if p)
                    if b == 0:
                        ew = ew + (t_bqk,)
                    first_ev = False
                ev_tok = op("scalar", lambda e, pt=pt, dst=dst, base=base, fc=fc, tt4=tt4: e.activation(
                    dst[:, base + tt4 * 512: base + (tt4 + 1) * 512], pt, AF.Identity,
                    bias=bqk_c[:, fc:fc + 1]), waits=ew, sig=True)
                psum_free[("pb0", k)] = ev_tok
        proj_mm_last[b] = last
        qk_evac_last[b] = ev_tok

    def emit_scores(b, stats_b=None):
        R = reg(b)
        exp_tok = vs_tok = last = None
        exp_l, vs_l = [], []
        for jc in range(NG):
            pt = pb1 if jc % 2 else pb0
            nm_ = "pb1" if jc % 2 else "pb0"
            mmw = psum_wait((nm_, 0), (nm_, 1), (nm_, 2), (nm_, 3))
            if jc == 0:
                mmw = mmw + (qk_evac_last[b],)
            for dc in range(DC):
                for ii in range(4):
                    last = op("tensor", lambda e, pt=pt, dc=dc, ii=ii, jc=jc, R=R: e.matmul(
                        pt[:, ii * 512:(ii + 1) * 512],
                        R["kT"][:, dc * 2048 + jc * 128: dc * 2048 + (jc + 1) * 128],
                        R["qT"][:, dc * 2048 + ii * 512: dc * 2048 + (ii + 1) * 512],
                        start=(dc == 0), stop=(dc == DC - 1)),
                        waits=mmw if (dc == 0 and ii == 0) else (), sig=(dc == DC - 1 and ii == 3))
            nm = scr[:, 80 + jc:81 + jc]
            sm = scr[:, 96 + jc:97 + jc]
            rq = scr[:, 112 + jc:113 + jc]
            op("vector", lambda e, pt=pt, nm=nm: e.reduce_max(nm, pt[:, 0:2048], axis=AX.X, negate=True),
               waits=(last,))
            mx2 = op("vector", lambda e, nm=nm: e.tensor_scalar_mul(nm, nm, 1.0 / D), sig=True)
            ew = (mx2,) + ((proj_mm_last[b],) if jc == 0 else ())
            exp_tok = op("scalar", lambda e, pt=pt, jc=jc, nm=nm, sm=sm, R=R: e.activation(
                R["st"][:, jc * 2048:(jc + 1) * 2048], pt[:, 0:2048], AF.Exp, bias=nm,
                scale=1.0 / D, accum_out=sm), waits=ew, sig=True)
            for k in range(4):
                psum_free[((nm_, k))] = exp_tok
            rc_ = op("vector", lambda e, sm=sm, rq=rq: e.reciprocal(rq, sm), waits=(exp_tok,), sig=True)
            vs_tok = op("scalar", lambda e, jc=jc, rq=rq: e.activation(
                A_v[:, jc * D:(jc + 1) * D], A_v[:, jc * D:(jc + 1) * D], AF.Identity, scale=rq),
                waits=(rc_,), sig=True)
            exp_l.append(exp_tok)
            vs_l.append(vs_tok)
            if stats_b is not None:
                emit_stats_group(stats_b, jc)
            for _ in range(2):
                if dq_cursor[0] < len(dq_all):
                    kind, a, j = dq_all[dq_cursor[0]]
                    emit_piece(kind, a, j)
                    dq_cursor[0] += 1
        scores_mm_last[b] = last
        return exp_l, vs_l

    def emit_attnv(b, dvcs, exp_l, vs_l, weave=None):
        R = reg(b)
        at_bufs = [A_at0, A_at1]
        last_mm = wrt = None
        for dvc in dvcs:
            if dvc % 2 == 0:
                sl = [pb0[:, 0:512], pb0[:, 512:1024], pb0[:, 1024:1536], pb0[:, 1536:2048]]
                keys = [("pb0", 0), ("pb0", 1), ("pb0", 2), ("pb0", 3)]
            else:
                sl = [pb1[:, 0:512], pb1[:, 512:1024], pb1[:, 1024:1536], pb0[:, 1536:2048]]
                keys = [("pb1", 0), ("pb1", 1), ("pb1", 2), ("pb0", 3)]
            mmw = psum_wait(*keys[:3])
            b3w = psum_wait(keys[3])
            for jc in range(NG):
                for ii in range(4):
                    w_ = mmw if (jc == 0 and ii == 0) else (b3w if (jc == 0 and ii == 3) else ())
                    if dvc == 0 and ii == 0:
                        w_ = w_ + (exp_l[jc], vs_l[jc])
                    last_mm = op("tensor", lambda e, sl=sl, dvc=dvc, jc=jc, ii=ii, R=R: e.matmul(
                        sl[ii], A_v[:, jc * D + dvc * 128: jc * D + (dvc + 1) * 128],
                        R["st"][:, jc * 2048 + ii * 512: jc * 2048 + (ii + 1) * 512],
                        start=(jc == 0), stop=(jc == NG - 1)),
                        waits=w_, sig=(jc == NG - 1 and ii == 3))
            abuf = at_bufs[dvc % 2]
            ew = (last_mm,) + tuple(p for p in (attn_evac_prev.get(dvc % 2),) if p)
            e1 = op("scalar", lambda e, sl=sl, abuf=abuf: e.activation(
                abuf[:, 1536:2048], sl[3], AF.Identity), waits=ew, sig=True)
            psum_free[keys[3]] = e1
            if dvc % 2 == 0:
                e2 = op("scalar", lambda e, abuf=abuf: e.activation(
                    abuf[:, 0:1536], pb0[:, 0:1536], AF.Identity), sig=True)
            else:
                e2 = op("scalar", lambda e, abuf=abuf: e.activation(
                    abuf[:, 0:1536], pb1[:, 0:1536], AF.Identity), sig=True)
            for kk in keys[:3]:
                psum_free[kk] = e2
            wrt = dma("scalar", f"g_at{dvc % 2}", at_sp[b, dvc, :, :], abuf[:], waits=(e2, e1))
            attn_evac_prev[dvc % 2] = wrt
            if weave and dvc in weave:
                for fn in weave[dvc]:
                    fn()
        return last_mm, wrt

    st_state = {}
    w1_state = {}

    def emit_w1_load():
        w1_state["tok"] = dma("scalar", "s_w1", F_w1[:].rearrange("p (c f) -> p c f", f=FF),
                              w1_sp.rearrange("c p f -> p c f"),
                              waits=(scores_mm_last[BPC - 1],) + tuple(w_spill_last["w1"]))

    for g in range(NG):
        emit_stats_group(0, g)
    pa, pm = emit_stats(0)
    pa(); pm()
    emit_statchain(0)
    emit_xn_wt(0)
    for b in range(BPC):
        emit_vproj(b)
        emit_qkproj(b)
        exp_l, vs_l = emit_scores(b, stats_b=(b + 1 if b + 1 < BPC else None))
        weave = {}
        if b + 1 < BPC:
            pa, pm = emit_stats(b + 1)
            weave = {0: [pa, pm], 1: [lambda bb=b + 1: emit_statchain(bb)]}
        if b == BPC - 1:
            emit_w1_load()
        emit_attnv(b, range(0, 4), exp_l, vs_l, weave=weave)
        mm2, wrt = emit_attnv(b, range(4, 8), exp_l, vs_l)
        attnv_mm_last[b] = mm2
        attn_dma_last[b] = wrt
        if b + 1 < BPC:
            emit_xn_wt(b + 1)

    b3 = BPC - 1
    w1_ld = w1_state["tok"]
    w2_ld = [dma("scalar", f"s_w2{hc}", F_w2[:, hc * FF:(hc + 1) * FF],
                 w2_sp[hc], waits=(attnv_mm_last[b3],) + tuple(w_spill_last["w2"]))
             for hc in range(DC)]
    f_at_war = (attn_evac_prev[0], attn_evac_prev[1])

    xh_war = [(xn_last[b3],), (attnv_mm_last[b3],)]
    h_war = [None, attnv_mm_last[b3]]
    at_war = [(xn_last[b3],)] * 2
    u_war = [None, None]
    o_war = [None, None, None, None]
    yT_war = attnv_mm_last[b3]
    silu_done = None
    h_done = None
    fc2_mm = {}

    z_war = [None, None]
    bc_war = [None, None]
    fstate = {}

    def f_front(ci):
        b, tc = ci // NCH, ci % NCH
        t0 = tc * TCH
        i2 = ci % 2
        xh = [F_xh0, F_xh1][i2]
        xld = dma("sync", f"s_xh{i2}", xh[:].rearrange("p (c j) -> p c j", j=TCH),
                  x_fm[b].rearrange("(c p) t -> p c t", p=128)[:, :, t0:t0 + TCH],
                  waits=xh_war[i2])
        at = [F_at0, F_at1][i2]
        ald = dma("sync", f"s_at{i2}", at[:].rearrange("p (c j) -> p c j", j=TCH),
                  at_sp.rearrange("b c p t -> b p c t")[b][:, :, t0:t0 + TCH],
                  waits=at_war[i2])
        bc = [F_bc0, F_bc1][i2]
        bcw = (row_spill_tok[b],) + tuple(p for p in (bc_war[i2],) if p)
        if ci < 2:
            bcw = bcw + (INIT, f_at_war[i2])
        bcd = dma("gpsimd", "g_rb", bc[:].rearrange("p (q t) -> p q t", t=TCH),
                  bass.AP(sr_sp.tensor, b * 10240 + 2048 + t0,
                          [[0, 128], [2048, 4], [1, TCH]]), waits=bcw)
        hbuf = [F_h0, F_h1][i2]
        h_done = None
        for dc in range(DC):
            ui = dc % 2
            ub = [F_u0, F_u1][ui]
            wts = ((xld, bcd) if dc == 0 else ())
            wts = wts + tuple(p for p in (u_war[ui],) if p)
            op("vector", lambda e, ub=ub, dc=dc, xh=xh, bc=bc: e.tensor_mul(
                ub[:], xh[:, dc * TCH:(dc + 1) * TCH], bc[:, 512:768]), waits=wts)
            vt = op("vector", lambda e, ub=ub, bc=bc: e.tensor_sub(ub[:], ub[:], bc[:, 768:1024]), sig=True)
            hw = (tuple(p for p in (h_war[i2],) if p) if dc == 0 else ())
            u_war[ui] = op("vector", lambda e, ub=ub, dc=dc, hbuf=hbuf: e.tensor_scalar(
                hbuf[:, dc * TCH:(dc + 1) * TCH], ub[:], gam_c[:, dc:dc + 1],
                bet_c[:, dc:dc + 1], ALU.mult, ALU.add), waits=hw, sig=True)
            h_done = u_war[ui]
        fstate[ci] = (xh, at, bc, hbuf, xld, ald, h_done)

    def f_back(ci):
        nonlocal yT_war
        b, tc = ci // NCH, ci % NCH
        t0 = tc * TCH
        i2 = ci % 2
        xh, at, bc, hbuf, xld, ald, h_done = fstate.pop(ci)
        silu_done = None
        for fc in range(FC):
            k = fc % 3
            pt = pb0[:, k * 512: k * 512 + TCH]
            mmw = psum_wait(("pb0", k)) + ((h_done, w1_ld) if fc == 0 else ())
            last = None
            for dc in range(DC):
                last = op("tensor", lambda e, pt=pt, fc=fc, dc=dc, hbuf=hbuf: e.matmul(
                    pt, F_w1[:, dc * FF + fc * 128: dc * FF + (fc + 1) * 128],
                    hbuf[:, dc * TCH:(dc + 1) * TCH],
                    start=(dc == 0), stop=(dc == DC - 1)),
                    waits=mmw if dc == 0 else (), sig=(dc == DC - 1))
            sw = (last,) + (tuple(p for p in (yT_war,) if p) if fc == 0 else ())
            if ci == 0 and fc == 0:
                sw = sw + (t_b1,)
            silu_done = op("scalar", lambda e, pt=pt, fc=fc: e.activation(
                F_yT[:, fc * TCH:(fc + 1) * TCH], pt, AF.Silu, bias=b1_c[:, fc:fc + 1]),
                waits=sw, sig=True)
            psum_free[("pb0", k)] = silu_done

        for hc in range(DC):
            k = hc % 3
            pt = pb1[:, k * 512: k * 512 + TCH]
            mmw = psum_wait(("pb1", k)) + ((silu_done,) if hc == 0 else ())
            if ci == 0:
                mmw = mmw + (w2_ld[hc],)
            last = None
            for fc in range(FC):
                last = op("tensor", lambda e, pt=pt, hc=hc, fc=fc: e.matmul(
                    pt, F_w2[:, hc * FF + fc * 128: hc * FF + (fc + 1) * 128],
                    F_yT[:, fc * TCH:(fc + 1) * TCH],
                    start=(fc == 0), stop=(fc == FC - 1)), waits=mmw if fc == 0 else (),
                    sig=(fc == FC - 1))
            fc2_mm[hc] = last
            zi = hc % 2
            zb = [F_z0, F_z1][zi]
            zw = (last,) + tuple(p for p in (z_war[zi],) if p)
            z_tok = op("scalar", lambda e, zb=zb, pt=pt, hc=hc: e.activation(
                zb[:], pt, AF.Identity, scale=0.5, bias=halfb2[:, hc:hc + 1]),
                waits=zw, sig=True)
            psum_free[("pb1", k)] = z_tok
            oi = hc % 4
            ob = F_o[oi]
            ow = tuple(p for p in (o_war[oi],) if p)
            op("vector", lambda e, ob=ob, hc=hc, xh=xh, bc=bc: e.tensor_mul(
                ob[:], xh[:, hc * TCH:(hc + 1) * TCH], bc[:, 256:512]), waits=ow)
            op("vector", lambda e, ob=ob, bc=bc: e.tensor_sub(ob[:], ob[:], bc[:, 0:256]))
            aw = (ald,) if hc == 0 else ()
            op("vector", lambda e, ob=ob, hc=hc, at=at: e.tensor_add(
                ob[:], ob[:], at[:, hc * TCH:(hc + 1) * TCH]), waits=aw)
            o4 = op("vector", lambda e, ob=ob, zb=zb: e.tensor_add(ob[:], ob[:], zb[:]),
                    waits=(z_tok,), sig=True)
            z_war[zi] = o4
            o_war[oi] = dma("gpsimd", f"g_o{oi}", out_fm[b, hc * 128:(hc + 1) * 128, t0:t0 + TCH],
                            ob[:], waits=(o4,))
            if hc == DC - 1:
                xh_war[i2] = (o4,)
                at_war[i2] = (o4,)
                bc_war[i2] = o4
        yT_war = fc2_mm[DC - 1]
        h_war[i2] = fc2_mm[DC - 1]

    f_front(0)
    for ci in range(BPC * NCH):
        if ci + 1 < BPC * NCH:
            f_front(ci + 1)
        f_back(ci)

    for snm in sorted(dma_sems):
        if snm.startswith("g_"):
            emit("gpsimd", lambda e, SS, s=snm, t=cnt[snm]: e.wait_ge(SS[s], t))

    sems = {}
    for snm in sorted(dma_sems) + ["mm", "vx", "ac"]:
        sems[snm] = nc.alloc_semaphore(name="sem_" + snm)
    with nc.Block() as blk:

        @blk.sync
        def _(e):
            for f in prog["sync"]:
                f(e, sems)

        @blk.gpsimd
        def _(e):
            for f in prog["gpsimd"]:
                f(e, sems)

        @blk.tensor
        def _(e):
            for f in prog["tensor"]:
                f(e, sems)

        @blk.vector
        def _(e):
            for f in prog["vector"]:
                f(e, sems)

        @blk.scalar
        def _(e):
            for f in prog["scalar"]:
                f(e, sems)

    return nc


def kernel(L0_attn_x, L0_attn_W, L0_attn_b, L0_ffn_ln_gamma, L0_ffn_ln_beta,
           L0_ffn_fc1_q, L0_ffn_fc1_scales, L0_ffn_fc1_bias, L0_ffn_fc2_q,
           L0_ffn_fc2_scales, L0_ffn_fc2_bias):
    x = np.asarray(L0_attn_x)
    W = np.asarray(L0_attn_W)
    bqk_full = np.asarray(L0_attn_b)
    WT = np.ascontiguousarray(W.T)
    bqk = np.ascontiguousarray(bqk_full[0:2048].reshape(16, 128).T)
    bv = np.ascontiguousarray(bqk_full[2048:3072].reshape(1, 1024))
    gam = np.ascontiguousarray(np.asarray(L0_ffn_ln_gamma).reshape(DC, 128).T)
    bet = np.ascontiguousarray(np.asarray(L0_ffn_ln_beta).reshape(DC, 128).T)
    q1 = np.ascontiguousarray(np.asarray(L0_ffn_fc1_q))
    sc1 = np.ascontiguousarray(np.repeat(np.asarray(L0_ffn_fc1_scales), 128, axis=0))
    b1 = np.ascontiguousarray(np.asarray(L0_ffn_fc1_bias).reshape(FC, 128).T)
    q2 = np.ascontiguousarray(np.asarray(L0_ffn_fc2_q))
    sc2 = np.ascontiguousarray(np.repeat(np.asarray(L0_ffn_fc2_scales), 128, axis=0))
    b2 = np.ascontiguousarray(np.asarray(L0_ffn_fc2_bias).reshape(DC, 128).T)
    ident = np.eye(128, dtype=np.float32)

    nc = build_program()
    in_maps = []
    for c in range(NCORE):
        xs = x[c * BPC:(c + 1) * BPC]
        in_maps.append({
            "x_tm": np.ascontiguousarray(xs),
            "x_fm": np.ascontiguousarray(xs.transpose(0, 2, 1)),
            "WT": WT, "bqk": bqk, "bv": bv, "gam": gam, "bet": bet,
            "q1": q1, "sc1": sc1, "b1": b1, "q2": q2, "sc2": sc2, "b2": b2,
            "ident": ident,
        })
    res = run_bass_kernel_spmd(nc, in_maps, core_ids=list(range(NCORE)))
    outs = [r["out_fm"].transpose(0, 2, 1) for r in res.results]
    return np.ascontiguousarray(np.concatenate(outs, axis=0))
